# version 01 (final)
# baseline (speedup 1.0000x reference)
import numpy as np
import concourse.bass as bass
import concourse.mybir as mybir
from concourse.bass_utils import run_bass_kernel_spmd
from contextlib import ExitStack

F32 = mybir.dt.float32
BF16 = mybir.dt.bfloat16
AF = mybir.ActivationFunctionType
ALU = mybir.AluOpType
AX = mybir.AxisListType.X

S, D, DI, NH, HD, DS, CH, TH = 2048, 1024, 2048, 32, 64, 64, 64, 32
DMA_SEMS = ("dCst", "dWa", "dWb", "dWoa", "dWob", "dGa", "dGb", "dUa", "dUb",
            "dDa", "dDb", "dF1a", "dF1b", "dF2a", "dF2b", "dFha", "dFhb",
            "dXa", "dXb", "dS1", "dS3", "gMis", "gPw", "gPo", "gSp", "gOut",
            "gTap")
DMLP, DHID, PROJ = 2560, 4096, 9248
OZ, OX, OB, OC, ODT, OTH = 0, 2048, 4096, 6144, 8192, 8224
NT = S // 128
NFB, FB = 4, 512

DEBUG_TAPS = {}
PHASES = ("prep", "mamba", "a3", "ffn")
FFN_UPTO = 9
FFN_BLOCKS = NFB


def host_consts():
    k = np.arange(128)[:, None]
    m = np.arange(128)[None, :]
    c = {
        "id128f": np.eye(128, dtype=np.float32),
        "u128": (k <= m).astype(np.float32),
        "u64bd": (((k // 64) == (m // 64)) & ((k % 64) <= (m % 64))).astype(np.float32),
        "ones": np.ones((128, 128), dtype=np.float32),
        "ramp64": (63.0 - (np.arange(128) % 64)).astype(np.float32)[:, None],
    }
    s127 = np.zeros((128, 128), dtype=np.float32); s127[127, :] = 1.0
    c["sel127"] = s127
    s63 = np.zeros((128, 128), dtype=np.float32)
    s63[63, :64] = 1.0; s63[127, 64:] = 1.0
    c["sel63"] = s63
    s63f = np.zeros((128, 128), dtype=np.float32)
    s63f[127, :64] = 1.0; s63f[63, 64:] = 1.0
    c["sel63f"] = s63f
    return c


class KB:

    def __init__(self, nc):
        self.nc = nc
        self.prog = {"sync": [], "tensor": [], "vector": [], "scalar": [], "gpsimd": []}
        self.sem = {}
        self.cnt = {}
        self.serial = {"vector": "sDV", "scalar": "sAC"}

    def new_sem(self, ctx, name):
        self.sem[name] = ctx.enter_context(self.nc.semaphore(name))
        self.cnt[name] = 0

    def _wrap(self, eng, emit, waits, inc, n):
        wl = [w for w in waits if w is not None]
        ser = self.serial.get(eng)
        if ser is not None:
            if self.cnt[ser] > 0:
                wl.append((self.sem[ser], self.cnt[ser]))
            if inc is None:
                inc = ser
        tok = None
        if inc is not None:
            self.cnt[inc] += n
            tok = (self.sem[inc], self.cnt[inc])
        sem_h = self.sem[inc] if inc is not None else None

        def f(e, emit=emit, wl=wl, sem_h=sem_h, n=n):
            for (h, v) in wl:
                e.wait_ge(h, v)
            ins = emit(e)
            if sem_h is not None:
                ins.then_inc(sem_h, n)
        self.prog[eng].append(f)
        return tok

    def dma(self, eng, out, in_, waits=(), slow=False, sem=None):
        if sem is None:
            sem = "dCst" if eng == "sync" else "gMis"
        def emit(e, out=out, in_=in_, slow=slow):
            if slow:
                return e.dma_start(out=out, in_=in_, allow_slow_non_contiguous=True)
            return e.dma_start(out=out, in_=in_)
        return self._wrap(eng, emit, waits, sem, 16)

    def mm(self, out, lhsT, rhs, start, stop, waits=(), inc=None, tp=None):
        def emit(e, out=out, lhsT=lhsT, rhs=rhs, start=start, stop=stop, tp=tp):
            return e.matmul(out, lhsT, rhs, start=start, stop=stop, tile_position=tp)
        return self._wrap("tensor", emit, waits, "sPE" if inc else None, 1)

    def tr(self, out, in_, ident, waits=(), inc=None):
        def emit(e, out=out, in_=in_, ident=ident):
            return e.transpose(out, in_, ident)
        return self._wrap("tensor", emit, waits, "sPE" if inc else None, 1)

    def tt(self, out, a, b, op, waits=(), inc=None):
        def emit(e, out=out, a=a, b=b, op=op):
            return e.tensor_tensor(out, a, b, op)
        return self._wrap("vector", emit, waits, "sDV" if inc else None, 1)

    def ts(self, out, in0, s1, s2, op0, op1=ALU.bypass, waits=(), inc=None):
        def emit(e, out=out, in0=in0, s1=s1, s2=s2, op0=op0, op1=op1):
            if s2 is None:
                return e.tensor_scalar(out, in0, s1, None, op0)
            return e.tensor_scalar(out, in0, s1, s2, op0, op1)
        return self._wrap("vector", emit, waits, "sDV" if inc else None, 1)

    def stt(self, out, in0, scalar, in1, op0, op1, waits=(), inc=None):
        def emit(e, out=out, in0=in0, scalar=scalar, in1=in1, op0=op0, op1=op1):
            return e.scalar_tensor_tensor(out, in0, scalar, in1, op0, op1)
        return self._wrap("vector", emit, waits, "sDV" if inc else None, 1)

    def cp(self, out, in_, waits=(), inc=None):
        return self._wrap("vector", lambda e, out=out, in_=in_: e.tensor_copy(out, in_),
                          waits, "sDV" if inc else None, 1)

    def red(self, out, in_, waits=(), inc=None):
        def emit(e, out=out, in_=in_):
            return e.tensor_reduce(out, in_, AX, ALU.add)
        return self._wrap("vector", emit, waits, "sDV" if inc else None, 1)

    def recip(self, out, in_, waits=(), inc=None):
        return self._wrap("vector", lambda e, out=out, in_=in_: e.reciprocal(out, in_),
                          waits, "sDV" if inc else None, 1)

    def act(self, out, in_, func, bias=0.0, scale=1.0, accum=None, waits=(), inc=None):
        def emit(e, out=out, in_=in_, func=func, bias=bias, scale=scale, accum=accum):
            return e.activation(out, in_, func, bias=bias, scale=scale, accum_out=accum)
        return self._wrap("scalar", emit, waits, "sAC" if inc else None, 1)

    def mset(self, ap, v, waits=(), inc=None):
        return self._wrap("vector", lambda e, ap=ap, v=v: e.memset(ap, v),
                          waits, "sDV" if inc else None, 1)

    def gate(self, eng, waits):
        self._wrap(eng, lambda e: e.nop(nofuse=True), waits, None, 1)

    def barrier(self):
        toks = [(self.sem[n], self.cnt[n])
                for n in ("sPE", "sDV", "sAC") + DMA_SEMS]
        for eng in ("sync", "tensor", "vector", "scalar", "gpsimd"):
            self.gate(eng, toks)

    def finalize(self, blk):
        for eng in ("sync", "tensor", "vector", "scalar", "gpsimd"):
            fns = self.prog[eng]
            runner = (lambda fns: (lambda e: [f(e) for f in fns]))(fns)
            getattr(blk, eng)(runner)


def b3(ap2d, P, A, Bn, axis=2):
    if axis == 2:
        return ap2d.unsqueeze(2).broadcast_to([P, A, Bn])
    return ap2d.unsqueeze(1).broadcast_to([P, A, Bn])


def flat(ap3):
    return ap3.rearrange("p a b -> p (a b)")


class _Taps:
    def __init__(self, k, g):
        self.k, self.g = k, g
        self.done = set()
        self.last = None

    def __call__(self, name, src, waits=()):
        if ("dbg_" + name) in self.g and name not in self.done:
            self.done.add(name)
            self.last = self.k.dma("gpsimd", self.g["dbg_" + name], src,
                                   waits=waits, sem="gTap")


def build_nc():
    nc = bass.Bass("TRN2", target_bir_lowering=False, detect_race_conditions=False)
    g = {}

    def din(name, shape, dt=F32):
        g[name] = nc.dram_tensor(name, list(shape), dt, kind="ExternalInput").ap()

    def dout(name, shape, dt=F32):
        g[name] = nc.dram_tensor(name, list(shape), dt, kind="ExternalOutput").ap()

    def dscr(name, shape, dt=F32):
        g[name] = nc.dram_tensor(name, list(shape), dt, kind="Internal").ap()

    din("x", [S, D])
    din("W_in", [D, PROJ]); din("W_out", [DI, D])
    din("Wg", [D, DMLP]); din("Wu", [D, DMLP]); din("Wd", [DMLP, D])
    din("norm1_w", [1, D]); din("norm2_w", [1, D])
    din("A_log", [1, NH]); din("D_param", [1, NH]); din("dt_bias", [1, NH])
    din("B_bias", [1, NH * DS]); din("C_bias", [1, NH * DS])
    din("Bn_w", [1, DS]); din("Cn_w", [1, DS])
    din("w1_w", [DHID, D]); din("w2_w", [DHID, D])
    din("h1_w", [DHID, DHID]); din("h2_w", [DHID, DHID]); din("w3_w", [D, DHID])
    din("w1_s", [1, DHID]); din("w2_s", [1, DHID])
    din("h1_s", [1, DHID]); din("h2_s", [1, DHID]); din("w3_s", [1, D])
    for n in ("id128f", "u128", "u64bd", "sel127", "sel63", "sel63f", "ones"):
        din(n, [128, 128])
    din("id128b", [128, 128], BF16)
    din("ramp64", [128, 1])
    dout("out", [S, D])
    for name, shape in DEBUG_TAPS.items():
        dout("dbg_" + name, shape)

    dscr("xnT_d", [128, 8, S])
    dscr("x2_d", [128, 8, S])
    dscr("s_d", [128, 8, S])
    dscr("w1T_d", [DHID // 128, 128, 8, 128], BF16)
    dscr("w2T_d", [DHID // 128, 128, 8, 128], BF16)
    dscr("h1T_d", [DHID // 128, 128, 32, 128], BF16)
    dscr("h2T_d", [DHID // 128, 128, 32, 128], BF16)
    dscr("w3T_d", [D // 128, 128, 32, 128], BF16)
    dscr("arow_d", [1, NH])

    with ExitStack() as ctx:
        def sb(scope, name, shape, dt=F32):
            g[name] = scope.enter_context(nc.sbuf_tensor(name, list(shape), dt))

        def psum(name, shape):
            g[name] = ctx.enter_context(nc.psum_tensor(name, list(shape), F32))

        for n, shp, dt in (("c_id", [128, 128], F32), ("c_idb", [128, 128], BF16),
                           ("c_u128", [128, 128], F32), ("c_u64", [128, 128], F32),
                           ("c_s127", [128, 128], F32), ("c_s63", [128, 128], F32),
                           ("c_s63f", [128, 128], F32), ("c_ones", [128, 128], F32),
                           ("c_ramp", [128, 1], F32), ("c_one", [128, 1], F32),
                           ("c_eps5", [128, 1], F32),
                           ("n1w", [128, 8], F32), ("n2w", [128, 8], F32),
                           ("w1s", [128, 32], F32), ("w2s", [128, 32], F32),
                           ("h1s", [128, 32], F32), ("h2s", [128, 32], F32),
                           ("w3s", [128, 8], F32),
                           ("dtb_b", [128, NH], F32), ("nega_b", [128, NH], F32),
                           ("dp_b", [128, NH], F32),
                           ("bnw_b", [128, DS], F32), ("cnw_b", [128, DS], F32),
                           ("bbias_b", [128, NH * DS], F32),
                           ("cbias_b", [128, NH * DS], F32),
                           ("arow", [1, NH], F32),
                           ("xt", [128, D], F32), ("xt2", [128, D], F32)):
            sb(ctx, n, shp, dt)

        psum("pA0", [128, 512]); psum("pA1", [128, 512]); psum("pA2", [128, 512])
        psum("pW", [128, 512]); psum("pT", [128, 512])
        psum("pU", [128, 1024]); psum("pS", [128, 512])

        k = KB(nc)
        for n in ("sPE", "sDV", "sAC") + DMA_SEMS:
            k.new_sem(ctx, n)
        blk = ctx.enter_context(nc.Block())
        taps = _Taps(k, g)
        st = {"x2_spills": [None] * NT, "xnT_spills": [None] * NT,
              "s_spills": [None] * NFB}

        _prelude(k, g)

        with ExitStack() as sc:
            sb(sc, "wsrc", [128, DHID], BF16)
            sb(sc, "pslab", [128, 32, 128], BF16)
            if "prep" in PHASES:
                _ffn_weight_prep(k, g)
        k.barrier()

        with ExitStack() as sc:
            sb(sc, "w_a", [128, 8, 512]); sb(sc, "w_b", [128, 8, 512])
            sb(sc, "xn_t", [128, D])
            sb(sc, "xnT", [128, 8, 128]); sb(sc, "hT", [128, 8, 128])
            sb(sc, "rs1", [128, 1]); sb(sc, "stat", [128, 8])
            sb(sc, "cum_a", [128, 1024]); sb(sc, "cum_b", [128, 1024])
            sb(sc, "tr_a", [128, 1024]); sb(sc, "tr_b", [128, 1024])
            sb(sc, "tr_c", [128, 1024])
            sb(sc, "sq", [128, 2048])
            sb(sc, "braw", [128, DI]); sb(sc, "brp", [128, DI]); sb(sc, "crp", [128, DI])
            sb(sc, "ct", [128, 16, 128])
            sb(sc, "xg", [128, DI]); sb(sc, "xd", [128, DI])
            sb(sc, "rhs2", [128, 16 * 128])
            sb(sc, "ybuf", [128, DI])
            sb(sc, "yT", [128, 16, 128])
            sb(sc, "wo_a", [128, 16, 128]); sb(sc, "wo_b", [128, 16, 128])
            sb(sc, "x2t", [128, 8, 128])
            sb(sc, "tmp64", [128, 64])
            for q in ("q_dt", "q_A", "q_gam", "q_acum", "q_expA", "q_expnA",
                      "q_cex", "q_rsl", "q_ds", "q_decp", "q_decf", "q_sk", "q_s2"):
                sb(sc, q, [128, NH])
            g["theta"] = None
            g["sz"] = g["crp"]
            k.mset(g["rhs2"][:, :], 0.0)
            k.mset(g["cum_b"][:, :], 0.0)
            ph = {}
            if "mamba" in PHASES:
                for t in range(NT):
                    _mamba_tile(k, g, t, st, ph, taps)
        k.barrier()

        with ExitStack() as sc:
            sb(sc, "slab1", [128, 8, 512]); sb(sc, "slab2", [128, 8, 512])
            sb(sc, "slab3", [128, 8, 512])
            sb(sc, "gate", [128, 20, 512])
            sb(sc, "wg_a", [128, 8, 128]); sb(sc, "wg_b", [128, 8, 128])
            sb(sc, "wu_a", [128, 8, 128]); sb(sc, "wu_b", [128, 8, 128])
            sb(sc, "wd_a", [128, 20, 128]); sb(sc, "wd_b", [128, 20, 128])
            sb(sc, "r2", [128, 512]); sb(sc, "tmpw", [128, 512]); sb(sc, "tmpw2", [128, 512])
            ph = {}
            if "a3" in PHASES:
                for b in range(NFB):
                    _a3_block(k, g, b, st, ph, taps)
        k.barrier()

        with ExitStack() as sc:
            g["slab1"] = sc.enter_context(nc.sbuf_tensor("fslab1", [128, 8, 512], F32))
            g["slab2"] = sc.enter_context(nc.sbuf_tensor("fslab2", [128, 8, 512], F32))
            g["slab3"] = sc.enter_context(nc.sbuf_tensor("fslab3", [128, 8, 512], F32))
            sb(sc, "xh", [128, 8, 512], BF16); sb(sc, "xl", [128, 8, 512], BF16)
            sb(sc, "t1h", [128, 32, 512], BF16); sb(sc, "t1l", [128, 32, 512], BF16)
            sb(sc, "f_a", [128, 32, 128], BF16); sb(sc, "f_b", [128, 32, 128], BF16)
            sb(sc, "fw1_a", [128, 8, 128], BF16); sb(sc, "fw1_b", [128, 8, 128], BF16)
            sb(sc, "fw2_a", [128, 8, 128], BF16); sb(sc, "fw2_b", [128, 8, 128], BF16)
            g["tmpw"] = sc.enter_context(nc.sbuf_tensor("ftmpw", [128, 512], F32))
            g["tmpw2"] = sc.enter_context(nc.sbuf_tensor("ftmpw2", [128, 512], F32))
            sb(sc, "outsb", [128, D])
            units = []
            for donor in ("slab1", "slab2", "slab3"):
                dv = g[donor][:, :, :].rearrange("p a b -> p (a b)").bitcast(BF16)
                for j in range(16):
                    units.append(dv[:, j * 512:(j + 1) * 512])
            for donor in ("xh", "xl"):
                for j in range(8):
                    units.append(g[donor][:, j, :])
            g["u1h_units"] = units[0:32]
            g["u1l_units"] = units[32:64]
            g["prod"] = g["slab1"]
            ph = {}
            if "ffn" in PHASES:
                for b in range(FFN_BLOCKS):
                    _ffn_block(k, g, b, st, ph, taps)

        k._wrap("gpsimd", lambda e: e.nop(nofuse=True),
                [(k.sem[n], k.cnt[n]) for n in
                 ("gMis", "gPw", "gPo", "gSp", "gOut", "gTap")], None, 1)
        k.finalize(blk)
    return nc, g


def _prelude(k, g):
    for src, dst in (("id128f", "c_id"), ("u128", "c_u128"), ("u64bd", "c_u64"),
                     ("sel127", "c_s127"), ("sel63", "c_s63"), ("sel63f", "c_s63f"),
                     ("ones", "c_ones"), ("ramp64", "c_ramp")):
        k.dma("sync", g[dst][:, :], g[src])
    k.dma("sync", g["c_idb"][:, :], g["id128b"])
    k.dma("sync", g["n1w"][:, :], g["norm1_w"].rearrange("o (kc p) -> p (o kc)", p=128),
          slow=True)
    k.dma("sync", g["n2w"][:, :], g["norm2_w"].rearrange("o (kc p) -> p (o kc)", p=128),
          slow=True)
    for src, dst in (("w1_s", "w1s"), ("w2_s", "w2s"), ("h1_s", "h1s"),
                     ("h2_s", "h2s"), ("w3_s", "w3s")):
        k.dma("sync", g[dst][:, :], g[src].rearrange("o (kc p) -> p (o kc)", p=128),
              slow=True)
    k.dma("sync", g["dtb_b"][:, :], g["dt_bias"].broadcast_to([128, NH]))
    k.dma("sync", g["dp_b"][:, :], g["D_param"].broadcast_to([128, NH]))
    k.dma("sync", g["bnw_b"][:, :], g["Bn_w"].broadcast_to([128, DS]))
    k.dma("sync", g["cnw_b"][:, :], g["Cn_w"].broadcast_to([128, DS]))
    k.dma("sync", g["bbias_b"][:, :], g["B_bias"].broadcast_to([128, NH * DS]))
    k.dma("sync", g["cbias_b"][:, :], g["C_bias"].broadcast_to([128, NH * DS]))
    t_alog = k.dma("sync", g["arow"][:, :], g["A_log"])
    ld_all = (k.sem["dCst"], k.cnt["dCst"])
    t_exp = k.act(g["arow"][0:1, :], g["arow"][0:1, :], AF.Exp, waits=[t_alog], inc=True)
    t_st = k.dma("gpsimd", g["arow_d"], g["arow"][0:1, :], waits=[t_exp])
    t_negb = k.dma("gpsimd", g["nega_b"][:, :],
                   g["arow_d"].broadcast_to([128, NH]), waits=[t_st])
    k.ts(g["nega_b"][:, :], g["nega_b"][:, :], -1.0, None, ALU.mult, waits=[t_negb])
    k.mset(g["c_one"][:, :], 1.0)
    t_dv = k.mset(g["c_eps5"][:, :], 1e-5, waits=[ld_all], inc=True)
    k.gate("scalar", [ld_all, t_dv])
    k.gate("tensor", [ld_all, t_dv])


def _ffn_weight_prep(k, g):
    ptb = g["pT"][:, :].bitcast(BF16)
    specs = [("w1_w", "w1T_d", DHID, D), ("w2_w", "w2T_d", DHID, D),
             ("h1_w", "h1T_d", DHID, DHID), ("h2_w", "h2T_d", DHID, DHID),
             ("w3_w", "w3T_d", D, DHID)]
    src_free = None
    pt_free = None
    slab_free = None
    for wname, tname, O, I in specs:
        kcN = I // 128
        slab = g["pslab"]
        for mo in range(O // 128):
            t_src = k.dma("gpsimd", g["wsrc"][:, 0:I],
                          g[wname][mo * 128:(mo + 1) * 128, :], waits=[src_free],
                          sem="gPw")
            evac_toks = []
            for k4 in range((kcN + 3) // 4):
                kc0, kc1 = k4 * 4, min(k4 * 4 + 4, kcN)
                tp_tok = None
                for kc in range(kc0, kc1):
                    w = [t_src, pt_free] if kc == kc0 else []
                    tp_tok = k.tr(ptb[:, (kc - kc0) * 128:(kc - kc0 + 1) * 128],
                                  g["wsrc"][:, kc * 128:(kc + 1) * 128],
                                  g["c_idb"][:, :], waits=w, inc=(kc == kc1 - 1))
                ev = k.cp(flat(slab[:, kc0:kc1, :]), ptb[:, 0:(kc1 - kc0) * 128],
                          waits=[tp_tok, slab_free if k4 == 0 else None], inc=True)
                pt_free = ev
                evac_toks.append(ev)
            src_free = evac_toks[-1]
            slab_free = k.dma("gpsimd", flat(g[tname][mo]), flat(slab[:, 0:kcN, :]),
                              waits=evac_toks, sem="gPo")


def _ln_tile(k, g, t, ph):
    xt = g["xt"] if t % 2 == 0 else g["xt2"]
    t_x = k.dma("sync", xt[:, :], g["x"][t * 128:(t + 1) * 128, :],
                waits=[ph.get("xt%d" % (t % 2))], sem="dXa" if t % 2 == 0 else "dXb")
    s_ = g["stat"]
    t_s1 = k.red(s_[:, 0:1], xt[:, :], waits=[t_x], inc=True)
    t_s2 = k.act(g["sq"][:, 0:D], xt[:, :], AF.Square, accum=s_[:, 1:2],
                 waits=[t_x, ph.get("stat_r"), ph.get("sq_own")], inc=True)
    k.ts(s_[:, 2:3], s_[:, 0:1], 1.0 / D, None, ALU.mult, waits=[t_s1])
    t_sr = k.ts(s_[:, 3:4], s_[:, 1:2], 1.0 / D, None, ALU.mult, waits=[t_s2], inc=True)
    ph["stat_r"] = t_sr
    k.tt(s_[:, 4:5], s_[:, 2:3], s_[:, 2:3], ALU.mult)
    t_v = k.tt(s_[:, 5:6], s_[:, 3:4], s_[:, 4:5], ALU.subtract, inc=True)
    t_sq = k.act(s_[:, 6:7], s_[:, 5:6], AF.Sqrt, bias=g["c_eps5"][:, 0:1],
                 waits=[t_v], inc=True)
    t_r = k.recip(s_[:, 6:7], s_[:, 6:7], waits=[t_sq], inc=True)
    t_xn = k.ts(g["xn_t"][:, :], xt[:, :], s_[:, 2:3], s_[:, 6:7],
                ALU.subtract, ALU.mult, waits=[t_r, ph.get("xn_pe")], inc=True)
    ph["xt%d" % (t % 2)] = t_xn
    t_q = k.act(g["sq"][:, 0:D], g["xn_t"][:, :], AF.Square, accum=s_[:, 7:8],
                waits=[t_xn], inc=True)
    t_q2 = k.ts(s_[:, 7:8], s_[:, 7:8], 1.0 / D, None, ALU.mult, waits=[t_q], inc=True)
    t_sq2 = k.act(g["rs1"][:, 0:1], s_[:, 7:8], AF.Sqrt, bias=g["c_eps5"][:, 0:1],
                  waits=[t_q2, ph.get("rs1_r")], inc=True)
    t_rs1 = k.recip(g["rs1"][:, 0:1], g["rs1"][:, 0:1], waits=[t_sq2], inc=True)
    ev_toks = []
    for half in range(2):
        tp_tok = None
        for kc in range(half * 4, half * 4 + 4):
            w = [t_xn, ph.get("pT")] if kc % 4 == 0 else []
            tp_tok = k.tr(g["pT"][:, (kc % 4) * 128:(kc % 4 + 1) * 128],
                          g["xn_t"][:, kc * 128:(kc + 1) * 128], g["c_id"][:, :],
                          waits=w, inc=(kc % 4 == 3))
        ev = k.cp(flat(g["xnT"][:, half * 4:half * 4 + 4, :]), g["pT"][:, 0:512],
                  waits=[tp_tok, ph.get("xnT_pe") if half == 0 else None], inc=True)
        ev_toks.append(ev)
        ph["pT"] = ev
    t_h = None
    for kc in range(8):
        t_h = k.ts(g["hT"][:, kc, :], g["xnT"][:, kc, :], g["n1w"][:, kc:kc + 1],
                   None, ALU.mult,
                   waits=[ph.get("hT_pe") if kc == 0 else None], inc=(kc == 7))
    return t_rs1, t_h, ev_toks, t_x


def _proj_block(k, g, ph, c0, ncols, writer, t_h):
    done = []
    off = 0
    while off < ncols:
        n = min(512, ncols - off)
        ph["wbuf_i"] = ph.get("wbuf_i", 0) ^ 1
        buf = ph["wbuf_i"]
        wslab = g["w_a"] if buf else g["w_b"]
        t_w = k.dma("sync", wslab[:, :, 0:n],
                    g["W_in"][:, c0 + off:c0 + off + n]
                    .rearrange("(kc p) c -> p kc c", p=128),
                    waits=[ph.get("wslab%d" % buf)], sem="dWa" if buf else "dWb")
        ph["pa_i"] = ph.get("pa_i", 0) + 1
        bank = ["pA0", "pA1", "pA2"][ph["pa_i"] % 3]
        pbank = g[bank]
        mm_tok = None
        for kc in range(8):
            w = [t_w, t_h, ph.get(bank)] if kc == 0 else []
            mm_tok = k.mm(pbank[:, 0:n], g["hT"][:, kc, :], wslab[:, kc, 0:n],
                          start=(kc == 0), stop=(kc == 7), waits=w, inc=(kc == 7))
        ph["wslab%d" % buf] = mm_tok
        ev = writer(pbank[:, 0:n], off, n, [mm_tok])
        ph[bank] = ev
        done.append(ev)
        off += n
    return done


def _mamba_tile(k, g, t, st, ph, taps):
    if taps.last is not None:
        k.gate("vector", [taps.last])
        k.gate("scalar", [taps.last])
        taps.last = None
    t_rs1, t_h, xnT_ev, t_x = _ln_tile(k, g, t, ph)
    taps("xn0", g["xn_t"][:, :], waits=[ph.get("xt%d" % (t % 2))])
    taps("xt0", (g["xt"] if t % 2 == 0 else g["xt2"])[:, :], waits=[ph.get("xt%d" % (t % 2))])
    taps("stat0", g["stat"][:, :], waits=[ph.get("xt%d" % (t % 2))])
    taps("ceps0", g["c_eps5"][:, :], waits=[ph.get("xt%d" % (t % 2))])
    taps("rs10", g["rs1"][:, :], waits=[t_rs1])
    st["xnT_spills"][t] = k.dma("gpsimd", g["xnT_d"][:, :, t * 128:(t + 1) * 128],
                                g["xnT"][:, :, :], waits=xnT_ev, sem="gSp")
    rs1c = g["rs1"][:, 0:1]
    theta = g["sq"][:, 1024:2048]
    q = {n: g[n][:, :] for n in ("q_dt", "q_A", "q_gam", "q_acum", "q_expA",
                                 "q_expnA", "q_cex", "q_rsl", "q_ds", "q_decp",
                                 "q_decf", "q_sk", "q_s2")}

    def w_dt(psv, off, n, w):
        return k.ts(q["q_dt"], psv[:, 0:NH], rs1c, None, ALU.mult,
                    waits=w + [t_rs1], inc=True)

    def w_th(psv, off, n, w):
        return k.ts(theta[:, off:off + n], psv, rs1c, None, ALU.mult,
                    waits=w, inc=True)
    d1 = _proj_block(k, g, ph, ODT, NH, w_dt, t_h)
    taps("qdt0", g["q_dt"][:, :], waits=d1)
    d2 = _proj_block(k, g, ph, OTH, NH * TH, w_th, t_h)

    t0 = k.tt(q["q_dt"], q["q_dt"], g["dtb_b"][:, :], ALU.add, waits=d1, inc=True)
    t1 = k.act(q["q_dt"], q["q_dt"], AF.Exp, waits=[t0, ph.get("qdt_sc")], inc=True)
    t2 = k.act(q["q_dt"], q["q_dt"], AF.Ln, bias=g["c_one"][:, 0:1], waits=[t1], inc=True)
    ph["qdt_sc"] = t2
    tA = k.tt(q["q_A"], q["q_dt"], g["nega_b"][:, :], ALU.mult, waits=[t2], inc=True)
    tal = k.act(q["q_s2"], q["q_A"], AF.Exp, waits=[tA, ph.get("qs2_dv")], inc=True)
    k.ts(q["q_gam"], q["q_A"], 1e-6, None, ALU.add, waits=[tA])
    k.recip(q["q_gam"], q["q_gam"])
    k.ts(q["q_s2"], q["q_s2"], -1.0, None, ALU.add, waits=[tal])
    k.tt(q["q_gam"], q["q_gam"], q["q_s2"], ALU.mult)
    tg3 = k.ts(q["q_gam"], q["q_gam"], 0.5, 1.0, ALU.mult, ALU.add, inc=True)
    tmmA = k.mm(g["pU"][:, 0:NH], g["c_u64"][:, :], q["q_A"], True, True,
                waits=[tA, ph.get("pU")], inc=True)
    tac = k.cp(q["q_acum"], g["pU"][:, 0:NH], waits=[tmmA], inc=True)
    texp = k.act(q["q_expA"], q["q_acum"], AF.Exp, waits=[tac], inc=True)
    texn = k.act(q["q_expnA"], q["q_acum"], AF.Exp, scale=-1.0, waits=[tac], inc=True)
    tmmB = k.mm(g["pU"][:, 512:512 + NH], g["c_u64"][:, :], q["q_expnA"], True, True,
                waits=[texn], inc=True)
    tcx = k.cp(q["q_cex"], g["pU"][:, 512:512 + NH], waits=[tmmB], inc=True)
    k.tt(q["q_rsl"], q["q_expA"], q["q_cex"], ALU.mult, waits=[texp])
    k.ts(q["q_rsl"], q["q_rsl"], g["c_ramp"][:, 0:1], None, ALU.add)
    tdp = k.mm(g["pU"][:, 0:NH], g["c_s63"][:, :], q["q_expA"], True, True,
               waits=[texp, tac], inc=True)
    tdp2 = k.cp(q["q_decp"], g["pU"][:, 0:NH], waits=[tdp], inc=True)
    tdf = k.mm(g["pU"][:, 512:512 + NH], g["c_s63f"][:, :], q["q_expA"], True, True,
               waits=[texp, tcx], inc=True)
    tdf2 = k.cp(q["q_decf"], g["pU"][:, 512:512 + NH], waits=[tdf], inc=True)
    tds = k.tt(q["q_ds"], q["q_decp"], q["q_expnA"], ALU.mult, waits=[texn], inc=True)

    cum_prev = g["cum_b" if t % 2 == 0 else "cum_a"]
    cum_cur = g["cum_a" if t % 2 == 0 else "cum_b"]
    prev_m2 = ph.get("cum_m2")
    tdth = k.tt(g["tr_a"][:, :].rearrange("p (h j) -> p h j", j=TH),
                theta.rearrange("p (h j) -> p h j", j=TH),
                b3(q["q_dt"], 128, NH, TH), ALU.mult,
                waits=d2 + [t2, ph.get("tr_pe")], inc=True)
    cev = []
    m2 = None
    for hf in range(2):
        k.mm(g["pU"][:, hf * 512:hf * 512 + 512], g["c_u128"][:, :],
             g["tr_a"][:, hf * 512:(hf + 1) * 512], True, False,
             waits=[tdth, tdf2] if hf == 0 else [])
        m2 = k.mm(g["pU"][:, hf * 512:hf * 512 + 512], g["c_s127"][:, :],
                  cum_prev[:, hf * 512:(hf + 1) * 512], False, True,
                  waits=[ph.get("cumprev")] if hf == 0 else [], inc=True)
        ev = k.cp(cum_cur[:, hf * 512:(hf + 1) * 512],
                  g["pU"][:, hf * 512:hf * 512 + 512],
                  waits=[m2, prev_m2 if hf == 0 else None], inc=True)
        cev.append(ev)
    ph["cum_m2"] = m2
    ph["cumprev"] = cev[-1]
    taps("cum0", cum_cur[:, :], waits=cev)

    tsh = k.act(g["tr_b"][:, :], cum_cur[:, :], AF.Sin, scale=1.0 / 64,
                waits=cev + [ph.get("trb_dv")], inc=True)
    tss = k.act(g["tr_a"][:, :], cum_cur[:, :], AF.Sin, scale=1.0 / 32,
                waits=[tdth], inc=True)
    tq_ = k.act(g["tr_b"][:, :], g["tr_b"][:, :], AF.Square, waits=[tsh], inc=True)
    tc = k.act(g["tr_c"][:, :], g["tr_b"][:, :], AF.Identity, bias=g["c_one"][:, 0:1],
               scale=-2.0, waits=[tq_, ph.get("trc_dv")], inc=True)
    sbuf, cbuf, tbuf = g["tr_a"], g["tr_c"], g["tr_b"]
    tok_s, tok_c = tss, tc
    for it in range(5):
        tns = k.stt(tbuf[:, :], sbuf[:, :], 2.0, cbuf[:, :], ALU.mult, ALU.mult,
                    waits=[tok_s, tok_c] + ([tq_] if it == 0 else []), inc=True)
        tnc = k.stt(sbuf[:, :], cbuf[:, :], 2.0, cbuf[:, :], ALU.mult, ALU.mult,
                    inc=True)
        tnc = k.ts(sbuf[:, :], sbuf[:, :], -1.0, None, ALU.add, inc=True)
        sbuf, cbuf, tbuf = tbuf, sbuf, cbuf
        tok_s, tok_c = tns, tnc
    cosb, sinb = cbuf, sbuf
    t_trig = (tok_s, tok_c)
    ph["trb_dv"] = tok_c
    ph["trc_dv"] = tok_c

    def w_braw(psv, off, n, w):
        return k.ts(g["braw"][:, off:off + n], psv, rs1c, None, ALU.mult,
                    waits=w, inc=True)
    taps("sin0", sinb[:, :], waits=[t_trig[0], t_trig[1]])
    bev = _proj_block(k, g, ph, OB, DI, w_braw, t_h)
    taps("braw0", g["braw"][:, :], waits=bev)
    t_brp = _bc_process(k, g, "braw", "brp", g["bnw_b"], g["bbias_b"], bev + [tdth],
                        cosb, sinb, t_trig, ph, "brp_pe")
    taps("b_roped", g["brp"][:, :], waits=[t_brp])

    def w_crp(psv, off, n, w):
        return k.ts(g["crp"][:, off:off + n], psv, rs1c, None, ALU.mult,
                    waits=w + [ph.get("crp_sc") if off == 0 else None], inc=True)
    cev2 = _proj_block(k, g, ph, OC, DI, w_crp, t_h)
    t_crp = _bc_process(k, g, "crp", "braw", g["cnw_b"], g["cbias_b"], cev2,
                        cosb, sinb, t_trig, ph, "braw_pe")
    ph["tr_pe"] = t_crp
    ct_ev = []
    for grp in range(4):
        tp_tok = None
        for j in range(grp * 4, grp * 4 + 4):
            w = [t_crp, ph.get("pT")] if j % 4 == 0 else []
            tp_tok = k.tr(g["pT"][:, (j % 4) * 128:(j % 4 + 1) * 128],
                          g["braw"][:, j * 128:(j + 1) * 128], g["c_id"][:, :],
                          waits=w, inc=(j % 4 == 3))
        ev = k.cp(flat(g["ct"][:, grp * 4:grp * 4 + 4, :]), g["pT"][:, 0:512],
                  waits=[tp_tok, ph.get("ct_pe") if grp == 0 else None], inc=True)
        ct_ev.append(ev)
        ph["pT"] = ev
    t_ct = ct_ev[-1]

    def w_x(psv, off, n, w):
        return k.ts(g["braw"][:, off:off + n], psv, rs1c, None, ALU.mult,
                    waits=w + [t_ct], inc=True)
    xev = _proj_block(k, g, ph, OX, DI, w_x, t_h)
    xr3 = g["braw"][:, :].rearrange("p (h d) -> p h d", d=HD)
    k.red(q["q_sk"].unsqueeze(2), xr3, waits=xev)
    tsk = k.tt(q["q_sk"], q["q_sk"], g["dp_b"][:, :], ALU.mult, inc=True)
    txg = k.tt(g["xg"][:, :].rearrange("p (h d) -> p h d", d=HD), xr3,
               b3(q["q_gam"], 128, NH, HD), ALU.mult,
               waits=[tg3, ph.get("xg_pe")], inc=True)
    txd = k.tt(g["xd"][:, :].rearrange("p (h d) -> p h d", d=HD),
               g["xg"][:, :].rearrange("p (h d) -> p h d", d=HD),
               b3(q["q_ds"], 128, NH, HD), ALU.mult,
               waits=[tds, ph.get("xd_pe")], inc=True)

    order = [(c, h) for c in (0, 1) for h in range(32)]
    mm1_tok = [None] * 64
    hs_tok = [None] * 64
    mm2_tok = [None] * 64
    y_tok = [None] * 64
    prev = ph.get("ssd_prev", {})
    m1banks = [g["pU"][:, 0:512], g["pU"][:, 512:1024], g["pS"][:, :], g["pA0"][:, :]]
    m2banks = [g["pA1"][:, :], g["pA2"][:, :], g["pW"][:, :], g["pT"][:, :]]
    m1keys = ["pU_a", "pU_b", "pS", "pA0"]
    m2keys = ["pA1", "pA2", "pW", "pT"]
    for idx, (c, h) in enumerate(order):
        par, pair = h & 1, h >> 1
        b1 = idx % 4
        pu = m1banks[b1][par * 64:(par + 1) * 64, 0:128]
        w = []
        if idx == 0:
            w = [txg, txd, t_brp, cev[-1], tdf2]
        if idx < 4:
            w.append(ph.get(m1keys[b1]))
            w.append(prev.get("m1_%d" % b1))
        else:
            w.append(hs_tok[idx - 4])
        bsl = g["brp"][c * 64:(c + 1) * 64, h * 64:(h + 1) * 64]
        k.mm(pu[:, 0:64], bsl, g["xg"][c * 64:(c + 1) * 64, h * 64:(h + 1) * 64],
             True, True, waits=w, tp=(c * 64, par * 64))
        mm1_tok[idx] = k.mm(pu[:, 64:128], bsl,
                            g["xd"][c * 64:(c + 1) * 64, h * 64:(h + 1) * 64],
                            True, True, tp=(c * 64, par * 64), inc=True)
        r2v = g["rhs2"][par * 64:(par + 1) * 64, pair * 128:pair * 128 + 128]
        k.cp(r2v[:, 0:64], pu[:, 0:64], waits=[mm1_tok[idx]])
        dect = g["q_decp"] if par == c else g["q_decf"]
        hs_tok[idx] = k.stt(r2v[:, 64:128], r2v[:, 64:128],
                            dect[par * 64:(par + 1) * 64, h:h + 1],
                            pu[:, 64:128], ALU.mult, ALU.add, inc=True)
        b2 = idx % 4
        psv = m2banks[b2][c * 64:(c + 1) * 64, 0:128]
        w2 = [hs_tok[idx]]
        if idx == 0:
            w2.append(t_ct)
        if idx < 4:
            w2.append(ph.get(m2keys[b2]))
            w2.append(prev.get("m2_%d" % b2))
        else:
            w2.append(y_tok[idx - 4])
        ctsl = g["ct"][par * 64:(par + 1) * 64, pair, c * 64:(c + 1) * 64]
        mm2_tok[idx] = k.mm(psv, ctsl, r2v[:, 0:128], True, True,
                            waits=w2, tp=(par * 64, c * 64), inc=True)
        rows = slice(c * 64, (c + 1) * 64)
        k.ts(g["tmp64"][rows, :], psv[:, 64:128], g["q_expA"][rows, h:h + 1],
             None, ALU.mult, waits=[mm2_tok[idx]])
        y_tok[idx] = k.stt(g["ybuf"][rows, h * 64:(h + 1) * 64], psv[:, 0:64],
                           g["q_rsl"][rows, h:h + 1], g["tmp64"][rows, :],
                           ALU.mult, ALU.add, inc=True)
    ph["ssd_prev"] = {("m1_%d" % i): hs_tok[60 + i] for i in range(4)}
    ph["ssd_prev"].update({("m2_%d" % i): y_tok[60 + i] for i in range(4)})
    for i, kk in enumerate(m1keys):
        ph[kk] = hs_tok[60 + i]
    for i, kk in enumerate(m2keys):
        ph[kk] = y_tok[60 + i]
    ph["pU"] = hs_tok[63]
    ph["ssd_prev"] = {("pu%d" % i): hs_tok[48 + i] for i in range(16)}
    ph["ssd_prev"].update({("ps%d" % i): y_tok[60 + i] for i in range(4)})
    ph["pU"] = hs_tok[63]
    ty = k.tt(g["ybuf"][:, :].rearrange("p (h d) -> p h d", d=HD),
              g["ybuf"][:, :].rearrange("p (h d) -> p h d", d=HD),
              b3(q["q_sk"], 128, NH, HD), ALU.add, waits=[tsk], inc=True)
    taps("y_ssd", g["ybuf"][:, :], waits=[ty])

    def w_z(psv, off, n, w):
        return k.act(g["sz"][:, off:off + n], psv, AF.Silu, scale=rs1c,
                     waits=w + ([t_rs1, t_crp] if off == 0 else []), inc=True)
    zev = _proj_block(k, g, ph, OZ, DI, w_z, t_h)
    tyz = k.tt(g["ybuf"][:, :], g["ybuf"][:, :], g["sz"][:, :], ALU.mult,
               waits=zev, inc=True)
    ph["xg_pe"] = mm1_tok[63]
    ph["xd_pe"] = mm1_tok[63]
    ph["brp_pe"] = mm1_tok[63]
    ph["ct_pe"] = mm2_tok[63]
    ph["braw_pe"] = t_ct

    yt_ev = []
    for grp in range(4):
        tp_tok = None
        for j in range(grp * 4, grp * 4 + 4):
            w = [tyz, ph.get("pT")] if j % 4 == 0 else []
            tp_tok = k.tr(g["pT"][:, (j % 4) * 128:(j % 4 + 1) * 128],
                          g["ybuf"][:, j * 128:(j + 1) * 128], g["c_id"][:, :],
                          waits=w, inc=(j % 4 == 3))
        ev = k.cp(flat(g["yT"][:, grp * 4:grp * 4 + 4, :]), g["pT"][:, 0:512],
                  waits=[tp_tok, ph.get("yT_pe") if grp == 0 else None], inc=True)
        yt_ev.append(ev)
        ph["pT"] = ev

    x2ev = []
    mt = None
    for mo in range(8):
        buf = mo % 2
        wsl = g["wo_a"] if buf == 0 else g["wo_b"]
        t_w = k.dma("sync", wsl[:, :, :],
                    g["W_out"][:, mo * 128:(mo + 1) * 128]
                    .rearrange("(kc p) c -> p kc c", p=128),
                    waits=[ph.get("wo%d" % buf)], sem="dWoa" if buf == 0 else "dWob")
        bank = "pA0" if mo % 2 == 0 else "pA1"
        pw = g[bank][:, 0:128]
        for kc in range(16):
            w = ([t_w, ph.get(bank)] + yt_ev) if kc == 0 else []
            mt = k.mm(pw, wsl[:, kc, :], g["yT"][:, kc, :],
                      start=(kc == 0), stop=(kc == 15), waits=w, inc=(kc == 15))
        ph["wo%d" % buf] = mt
        ev = k.tt(g["x2t"][:, mo, :], pw, g["xnT"][:, mo, :], ALU.add,
                  waits=[mt, ph.get("x2t_gp") if mo == 0 else None], inc=True)
        ph[bank] = ev
        x2ev.append(ev)
    ph["xnT_pe"] = x2ev[-1]
    ph["hT_pe"] = mt
    ph["xn_pe"] = x2ev[-1]
    ph["yT_pe"] = mt
    ph["rs1_r"] = x2ev[-1]
    t_sp = k.dma("gpsimd", g["x2_d"][:, :, t * 128:(t + 1) * 128],
                 g["x2t"][:, :, :], waits=x2ev, sem="gSp")
    st["x2_spills"][t] = t_sp
    ph["x2t_gp"] = t_sp


def _bc_process(k, g, src, dst, nw_b, bias_b, evs, cosb, sinb, t_trig, ph, pekey):
    q2 = g["q_s2"][:, :]
    tq = k.act(g["sq"][:, 0:DI], g[src][:, :], AF.Square,
               waits=evs + [ph.get("sq_own")], inc=True)
    k.red(q2.unsqueeze(2), g["sq"][:, 0:DI].rearrange("p (h n) -> p h n", n=DS),
          waits=[tq, ph.get("qs2_sc2")])
    tq2 = k.ts(q2, q2, 1.0 / DS, 1e-5, ALU.mult, ALU.add, inc=True)
    tsq = k.act(q2, q2, AF.Sqrt, waits=[tq2], inc=True)
    ph["qs2_sc2"] = tsq
    k.recip(q2, q2, waits=[tsq])
    trw = k.tt(g["sq"][:, 0:DI].rearrange("p (h n) -> p h n", n=DS),
               b3(q2, 128, NH, DS), b3(nw_b[:, :], 128, NH, DS, axis=1), ALU.mult,
               inc=True)
    ph["qs2_dv"] = trw
    k.tt(g[dst][:, :], g[src][:, :], g["sq"][:, 0:DI], ALU.mult,
         waits=[ph.get(pekey)])
    t2 = k.tt(g[src][:, :], g[dst][:, :], bias_b[:, :], ALU.add, inc=True)
    sev = g[src][:, :].rearrange("p (a two) -> p a two", two=2)[:, :, 0]
    sod = g[src][:, :].rearrange("p (a two) -> p a two", two=2)[:, :, 1]
    dev = g[dst][:, :].rearrange("p (a two) -> p a two", two=2)[:, :, 0]
    dod = g[dst][:, :].rearrange("p (a two) -> p a two", two=2)[:, :, 1]
    sq0, sq1 = g["sq"][:, 0:1024], g["sq"][:, 1024:2048]
    k.tt(sq0, cosb[:, :], sev, ALU.mult, waits=[t2, t_trig[0], t_trig[1]])
    k.tt(sq1, sinb[:, :], sod, ALU.mult)
    k.tt(dev, sq0, sq1, ALU.subtract)
    k.tt(sq0, sinb[:, :], sev, ALU.mult)
    k.tt(sq1, cosb[:, :], sod, ALU.mult)
    tod = k.tt(dod, sq0, sq1, ALU.add, inc=True)
    ph["sq_own"] = tod
    return tod


def _a3_block(k, g, b, st, ph, taps):
    if taps.last is not None:
        k.gate("vector", [taps.last])
        k.gate("scalar", [taps.last])
        taps.last = None
    t_x2 = k.dma("sync", g["slab1"][:, :, :],
                 g["x2_d"][:, :, b * FB:(b + 1) * FB],
                 waits=[ph.get("slab1_dv")], sem="dS1")
    mt = None
    for kc in range(8):
        tw = g["tmpw"] if kc % 2 == 0 else g["tmpw2"]
        tqk = k.act(tw[:, :], g["slab1"][:, kc, :], AF.Square,
                    waits=[t_x2, ph.get("tmpw%d" % (kc % 2))], inc=True)
        w = [tqk] + ([ph.get("pA0")] if kc == 0 else [])
        mt = k.mm(g["pA0"][0:1, 0:512], g["c_ones"][:, 0:1], tw[:, :],
                  start=(kc == 0), stop=(kc == 7), waits=w, inc=(kc == 7))
        ph["tmpw%d" % (kc % 2)] = mt
    tr0 = k.ts(g["r2"][0:1, :], g["pA0"][0:1, 0:512], 1.0 / D, 1e-5,
               ALU.mult, ALU.add, waits=[mt, ph.get("r2_pe")], inc=True)
    ph["pA0"] = tr0
    tsq = k.act(g["r2"][0:1, :], g["r2"][0:1, :], AF.Sqrt, waits=[tr0], inc=True)
    trc = k.recip(g["r2"][0:1, :], g["r2"][0:1, :], waits=[tsq], inc=True)
    tbc = k.mm(g["pA1"][:, 0:512], g["c_ones"][0:1, 0:128], g["r2"][0:1, :],
               True, True, waits=[trc, ph.get("pA1")], inc=True)
    tbc2 = k.cp(g["r2"][:, :], g["pA1"][:, 0:512], waits=[tbc], inc=True)
    ph["pA1"] = tbc2
    ph["r2_pe"] = tbc
    h2t = []
    for kc in range(8):
        h2t.append(k.stt(g["slab2"][:, kc, :], g["slab1"][:, kc, :],
                         g["n2w"][:, kc:kc + 1], g["r2"][:, :], ALU.mult, ALU.mult,
                         waits=[ph.get("slab2_pe") if kc == 0 else None], inc=True))
    taps("h2", flat(g["slab2"][:, :, :]), waits=[h2t[-1]])
    gate_toks = []
    mg = mu = None
    for mo in range(20):
        buf = mo % 2
        wgs = g["wg_a"] if buf == 0 else g["wg_b"]
        wus = g["wu_a"] if buf == 0 else g["wu_b"]
        twg = k.dma("sync", wgs[:, :, :],
                    g["Wg"][:, mo * 128:(mo + 1) * 128].rearrange("(kc p) c -> p kc c", p=128),
                    waits=[ph.get("wg%d" % buf)], sem="dGa" if buf == 0 else "dGb")
        twu = k.dma("sync", wus[:, :, :],
                    g["Wu"][:, mo * 128:(mo + 1) * 128].rearrange("(kc p) c -> p kc c", p=128),
                    waits=[ph.get("wu%d" % buf)], sem="dUa" if buf == 0 else "dUb")
        for kc in range(8):
            w = ([twg, ph.get("pA0")] + h2t) if kc == 0 else []
            mg = k.mm(g["pA0"][:, 0:512], wgs[:, kc, :], g["slab2"][:, kc, :],
                      start=(kc == 0), stop=(kc == 7), waits=w, inc=(kc == 7))
        for kc in range(8):
            w = [twu, ph.get("pA1")] if kc == 0 else []
            mu = k.mm(g["pA1"][:, 0:512], wus[:, kc, :], g["slab2"][:, kc, :],
                      start=(kc == 0), stop=(kc == 7), waits=w, inc=(kc == 7))
        ph["wg%d" % buf] = mg
        ph["wu%d" % buf] = mu
        tsl = k.act(g["tmpw"][:, :], g["pA0"][:, 0:512], AF.Silu,
                    waits=[mg, ph.get("tmpw0")], inc=True)
        ph["pA0"] = tsl
        tgt = k.tt(g["gate"][:, mo, :], g["tmpw"][:, :], g["pA1"][:, 0:512], ALU.mult,
                   waits=[tsl, mu, ph.get("gate_pe") if mo == 0 else None], inc=True)
        ph["pA1"] = tgt
        ph["tmpw0"] = tgt
        gate_toks.append(tgt)
    ph["slab2_pe"] = mu
    sev = []
    mt2 = None
    for mo in range(8):
        buf = mo % 2
        wds = g["wd_a"] if buf == 0 else g["wd_b"]
        twd = k.dma("sync", wds[:, :, :],
                    g["Wd"][:, mo * 128:(mo + 1) * 128].rearrange("(kc p) c -> p kc c", p=128),
                    waits=[ph.get("wd%d" % buf)], sem="dDa" if buf == 0 else "dDb")
        for kc in range(20):
            w = ([twd, ph.get("pW")] + gate_toks) if kc == 0 else []
            mt2 = k.mm(g["pW"][:, 0:512], wds[:, kc, :], g["gate"][:, kc, :],
                       start=(kc == 0), stop=(kc == 19), waits=w, inc=(kc == 19))
        ph["wd%d" % buf] = mt2
        ev = k.tt(g["slab3"][:, mo, :], g["pW"][:, 0:512], g["slab1"][:, mo, :],
                  ALU.add, waits=[mt2, ph.get("slab3_gp") if mo == 0 else None],
                  inc=True)
        ph["pW"] = ev
        sev.append(ev)
    ph["gate_pe"] = mt2
    ph["slab1_dv"] = sev[-1]
    taps("s_out", flat(g["slab3"][:, :, :]), waits=sev)
    t_sp = k.dma("gpsimd", g["s_d"][:, :, b * FB:(b + 1) * FB],
                 g["slab3"][:, :, :], waits=sev, sem="gSp")
    st["s_spills"][b] = t_sp
    ph["slab3_gp"] = t_sp


def _ffn_block(k, g, b, st, ph, taps):
    if taps.last is not None:
        k.gate("vector", [taps.last])
        k.gate("scalar", [taps.last])
        taps.last = None
    t_xn = k.dma("sync", g["slab1"][:, :, :],
                 g["xnT_d"][:, :, b * FB:(b + 1) * FB],
                 waits=[ph.get("slab1_pe")], sem="dS1")
    spl = []
    for kc in range(8):
        th = k.cp(g["xh"][:, kc, :], g["slab1"][:, kc, :],
                  waits=[t_xn] + ([ph.get("xh_pe")] if kc == 0 else []), inc=True)
        tl = k.tt(g["xl"][:, kc, :], g["slab1"][:, kc, :], g["xh"][:, kc, :],
                  ALU.subtract, inc=True)
        spl.append(tl)

    if FFN_UPTO < 1:
        return

    def two_pass(pb, wslab, hi_l, lo_l, kcN, extra_w):
        mt = None
        for p, srcl in ((0, hi_l), (1, lo_l)):
            for kc in range(kcN):
                first = (p == 0 and kc == 0)
                last = (p == 1 and kc == kcN - 1)
                mt = k.mm(pb, wslab[:, kc, :], srcl[kc],
                          start=first, stop=last,
                          waits=extra_w if first else [], inc=last)
        return mt

    xh_l = [g["xh"][:, kc, :] for kc in range(8)]
    xl_l = [g["xl"][:, kc, :] for kc in range(8)]
    t1h_l = [g["t1h"][:, m, :] for m in range(32)]
    t1l_l = [g["t1l"][:, m, :] for m in range(32)]

    t1_toks = []
    m1 = m2 = None
    if FFN_UPTO < 2:
        return
    taps("xh_t", g["xh"][:, :, :].rearrange("p a b -> p (a b)").bitcast(BF16).bitcast(BF16) if False else g["xh"][:, :, :].rearrange("p a b -> p (a b)"), waits=spl)
    for mo in range(32):
        buf = mo % 2
        w1s_ = g["fw1_a"] if buf == 0 else g["fw1_b"]
        w2s_ = g["fw2_a"] if buf == 0 else g["fw2_b"]
        tw1 = k.dma("sync", flat(w1s_[:, :, :]), flat(g["w1T_d"][mo]),
                    waits=[ph.get("fw1%d" % buf)], sem="dF1a" if buf == 0 else "dF1b")
        tw2 = k.dma("sync", flat(w2s_[:, :, :]), flat(g["w2T_d"][mo]),
                    waits=[ph.get("fw2%d" % buf)], sem="dF2a" if buf == 0 else "dF2b")
        if mo == 0:
            taps("w1T0", flat(w1s_[:, :, :]), waits=[tw1])
        m1 = two_pass(g["pA0"][:, 0:512], w1s_, xh_l, xl_l, 8,
                      [tw1, ph.get("pA0")] + spl)
        m2 = two_pass(g["pA1"][:, 0:512], w2s_, xh_l, xl_l, 8,
                      [tw2, ph.get("pA1")])
        ph["fw1%d" % buf] = m1
        ph["fw2%d" % buf] = m2
        ta = k.act(g["tmpw"][:, :], g["pA0"][:, 0:512], AF.Silu,
                   scale=g["w1s"][:, mo:mo + 1], waits=[m1, ph.get("tmpw0")], inc=True)
        ph["pA0"] = ta
        tb = k.stt(g["tmpw2"][:, :], g["pA1"][:, 0:512], g["w2s"][:, mo:mo + 1],
                   g["tmpw"][:, :], ALU.mult, ALU.mult,
                   waits=[ta, m2, ph.get("tmpw1")], inc=True)
        ph["pA1"] = tb
        ph["tmpw0"] = tb
        th = k.cp(g["t1h"][:, mo, :], g["tmpw2"][:, :],
                  waits=[ph.get("t1_pe") if mo == 0 else None], inc=True)
        tl = k.tt(g["t1l"][:, mo, :], g["tmpw2"][:, :], g["t1h"][:, mo, :],
                  ALU.subtract, inc=True)
        ph["tmpw1"] = tl
        t1_toks.append(tl)
    ph["xh_pe"] = m2
    taps("t1h_t", flat(g["t1h"][:, :, :]), waits=t1_toks)
    if FFN_UPTO < 3:
        return

    def mid_stage(tname, hi_in, lo_in, hi_out, lo_out, scol, func, in_toks, outkeys):
        out_toks = []
        mlast = None
        for mo in range(32):
            buf = mo % 2
            wsl = g["f_a"] if buf == 0 else g["f_b"]
            bank = "pA2" if buf == 0 else "pW"
            tw = k.dma("sync", flat(wsl[:, :, :]), flat(g[tname][mo]),
                       waits=[ph.get("fh%d" % buf), ph.pop("fh%d_tap" % buf, None)],
                       sem="dFha" if buf == 0 else "dFhb")
            if mo == 0 and tname == "h1T_d" and "dbg_h1T0" in g and "h1T0" not in taps.done:
                taps("h1T0", flat(wsl[:, :, :]), waits=[tw])
                ph["fh0_tap"] = taps.last
            mlast = two_pass(g[bank][:, 0:512], wsl, hi_in, lo_in, 32,
                             [tw, ph.get(bank)] + (in_toks if mo == 0 else []))
            ph["fh%d" % buf] = mlast
            ta = k.act(g["tmpw"][:, :], g[bank][:, 0:512], func,
                       scale=scol[:, mo:mo + 1], waits=[mlast, ph.get("tmpw0")],
                       inc=True)
            ph[bank] = ta
            ph["tmpw0"] = ta
            th = k.cp(hi_out[mo], g["tmpw"][:, :],
                      waits=[ta] + ([ph.get(ok) for ok in outkeys] if mo == 0 else []),
                      inc=True)
            tl = k.tt(lo_out[mo], g["tmpw"][:, :], hi_out[mo], ALU.subtract,
                      inc=True)
            ph["tmpw0"] = tl
            out_toks.append(tl)
        return out_toks, mlast

    u1_toks, ml1 = mid_stage("h1T_d", t1h_l, t1l_l, g["u1h_units"], g["u1l_units"],
                             g["h1s"], AF.Gelu, t1_toks,
                             ["xh_pe", "prod_pe"])
    ph["t1_pe"] = ml1
    if "dbg_u1h_t" in g and "u1h_t" not in taps.done:
        taps.done.add("u1h_t")
        for j in range(32):
            taps.last = k.dma("gpsimd", g["dbg_u1h_t"][:, j * 512:(j + 1) * 512],
                              g["u1h_units"][j], waits=[u1_toks[-1]], sem="gTap")
        ph["u1tap"] = taps.last
        ph["u1tap2"] = taps.last
    u2_toks, ml2 = mid_stage("h2T_d", g["u1h_units"], g["u1l_units"], t1h_l, t1l_l,
                             g["h2s"], AF.Gelu, u1_toks, ["t1_pe"])
    taps("u2h_t", flat(g["t1h"][:, :, :]), waits=u2_toks)
    if FFN_UPTO < 5:
        return
    if taps.last is not None and "u2h_t" in taps.done:
        ph["t1_pe"] = taps.last if ph.get("t1_pe") is None else ph["t1_pe"]
    g_toks = []
    m3 = None
    for mo in range(8):
        buf = mo % 2
        wsl = g["f_a"] if buf == 0 else g["f_b"]
        tw = k.dma("sync", flat(wsl[:, 0:32, :]), flat(g["w3T_d"][mo]),
                   waits=[ph.get("fh%d" % buf)], sem="dFha" if buf == 0 else "dFhb")
        m3 = two_pass(g["pA2"][:, 0:512], wsl, t1h_l, t1l_l, 32,
                      [tw, ph.get("pA2")] + (u2_toks if mo == 0 else []))
        ph["fh%d" % buf] = m3
        tg = k.act(g["slab2"][:, mo, :], g["pA2"][:, 0:512], AF.Silu,
                   scale=g["w3s"][:, mo:mo + 1],
                   waits=[m3] + ([ml2] if mo == 0 else []), inc=True)
        ph["pA2"] = tg
        g_toks.append(tg)
    ph["t1_pe"] = m3
    taps("g_out", flat(g["slab2"][:, :, :]), waits=[g_toks[-1]])

    if FFN_UPTO < 6:
        return
    t_s = k.dma("sync", g["slab3"][:, :, :],
                g["s_d"][:, :, b * FB:(b + 1) * FB],
                waits=[ml2, ph.get("slab3_gp"), ph.get("u1tap2")], sem="dS3")
    pr_toks = []
    if FFN_UPTO < 7:
        return
    for kc in range(8):
        pr_toks.append(k.tt(g["prod"][:, kc, :], g["slab2"][:, kc, :],
                            g["slab3"][:, kc, :], ALU.mult,
                            waits=[t_s, g_toks[kc],
                                   ph.get("prod_pe") if kc == 0 else None],
                            inc=True))
    ph["slab23_r"] = pr_toks[-1]
    tp_tok = None
    for tt_ in range(4):
        trow = b * FB + tt_ * 128
        t_x = k.dma("sync", g["xt"][:, :], g["x"][trow:trow + 128, :],
                    waits=[ph.get("xt_gp")], sem="dXa")
        ev_all = []
        for kc in range(8):
            bname = "pT" if kc % 2 == 0 else "pU"
            bank = g["pT"][:, 0:128] if kc % 2 == 0 else g["pU"][:, 0:128]
            w = [pr_toks[kc], ph.get(bname)]
            tp_tok = k.tr(bank, g["prod"][:, kc, tt_ * 128:(tt_ + 1) * 128],
                          g["c_id"][:, :], waits=w, inc=True)
            ev = k.tt(g["outsb"][:, kc * 128:(kc + 1) * 128], bank,
                      g["xt"][:, kc * 128:(kc + 1) * 128], ALU.add,
                      waits=[tp_tok, t_x,
                             ph.get("outsb_gp") if kc == 0 else None],
                      inc=True)
            ev_all.append(ev)
            ph[bname] = ev
        if FFN_UPTO >= 8:
            t_o = k.dma("gpsimd", g["out"][trow:trow + 128, :], g["outsb"][:, :],
                        waits=ev_all, sem="gOut")
        else:
            t_o = ev_all[-1]
        ph["outsb_gp"] = t_o
        ph["xt_gp"] = t_o
    ph["prod_pe"] = tp_tok
    ph["slab1_pe"] = tp_tok
    ph["slab3_gp"] = pr_toks[-1]


def kernel(**inputs):
    import ml_dtypes
    x_full = np.ascontiguousarray(np.asarray(inputs["L0_attn_x"], dtype=np.float32))
    consts = host_consts()
    feed = {
        "W_in": np.ascontiguousarray(inputs["L0_attn_W_in"], dtype=np.float32),
        "W_out": np.ascontiguousarray(inputs["L0_attn_W_out"], dtype=np.float32),
        "Wg": np.ascontiguousarray(inputs["L0_attn_Wg"], dtype=np.float32),
        "Wu": np.ascontiguousarray(inputs["L0_attn_Wu"], dtype=np.float32),
        "Wd": np.ascontiguousarray(inputs["L0_attn_Wd"], dtype=np.float32),
        "norm1_w": np.asarray(inputs["L0_attn_norm1_w"], dtype=np.float32).reshape(1, -1),
        "norm2_w": np.asarray(inputs["L0_attn_norm2_w"], dtype=np.float32).reshape(1, -1),
        "A_log": np.asarray(inputs["L0_attn_A_log"], dtype=np.float32).reshape(1, -1),
        "D_param": np.asarray(inputs["L0_attn_D_param"], dtype=np.float32).reshape(1, -1),
        "dt_bias": np.asarray(inputs["L0_attn_dt_bias"], dtype=np.float32).reshape(1, -1),
        "B_bias": np.asarray(inputs["L0_attn_B_bias"], dtype=np.float32).reshape(1, -1),
        "C_bias": np.asarray(inputs["L0_attn_C_bias"], dtype=np.float32).reshape(1, -1),
        "Bn_w": np.asarray(inputs["L0_attn_Bn_w"], dtype=np.float32).reshape(1, -1),
        "Cn_w": np.asarray(inputs["L0_attn_Cn_w"], dtype=np.float32).reshape(1, -1),
        "w1_w": np.ascontiguousarray(inputs["L0_ffn_w1_w"], dtype=np.float32),
        "w2_w": np.ascontiguousarray(inputs["L0_ffn_w2_w"], dtype=np.float32),
        "h1_w": np.ascontiguousarray(inputs["L0_ffn_h1_w"], dtype=np.float32),
        "h2_w": np.ascontiguousarray(inputs["L0_ffn_h2_w"], dtype=np.float32),
        "w3_w": np.ascontiguousarray(inputs["L0_ffn_w3_w"], dtype=np.float32),
        "w1_s": np.asarray(inputs["L0_ffn_w1_s"], dtype=np.float32).reshape(1, -1),
        "w2_s": np.asarray(inputs["L0_ffn_w2_s"], dtype=np.float32).reshape(1, -1),
        "h1_s": np.asarray(inputs["L0_ffn_h1_s"], dtype=np.float32).reshape(1, -1),
        "h2_s": np.asarray(inputs["L0_ffn_h2_s"], dtype=np.float32).reshape(1, -1),
        "w3_s": np.asarray(inputs["L0_ffn_w3_s"], dtype=np.float32).reshape(1, -1),
    }
    feed.update(consts)
    feed["id128b"] = consts["id128f"].astype(ml_dtypes.bfloat16)

    nc, _ = build_nc()
    n_cores = 8
    in_maps = []
    for c in range(n_cores):
        m = dict(feed)
        m["x"] = np.ascontiguousarray(x_full[c])
        in_maps.append(m)
    res = run_bass_kernel_spmd(nc, in_maps, core_ids=list(range(n_cores)))
    out = np.stack([r["out"] for r in res.results], axis=0)
    kernel._last_results = res
    return out
